# version 05
# baseline (speedup 1.0000x reference)
import contextlib
import numpy as np
import ml_dtypes
import concourse.bass as bass
import concourse.mybir as mybir
from concourse.bass_utils import run_bass_kernel_spmd
from concourse.bass_isa import ReduceOp
from concourse import library_config

F32 = mybir.dt.float32
F16 = mybir.dt.float16
BF16 = mybir.dt.bfloat16
AF = mybir.ActivationFunctionType
OP = mybir.AluOpType
AX = mybir.AxisListType

T = 2048
E = 1024
NT = 16
NF = 8
H = 16
D = 64
M = 256
FI = 4096
MAGIC = 1536.0

STAGES = 4
DEBUG = None

WEIGHT_KEYS = [
    "L0_attn_a_Wq", "L0_attn_a_Wk", "L0_attn_a_Wv", "L0_attn_a_Wo",
    "L0_attn_b_q_w", "L0_attn_b_q_b", "L0_attn_b_k_w", "L0_attn_b_k_b",
    "L0_attn_b_v_w", "L0_attn_b_v_b", "L0_attn_b_out_w", "L0_attn_b_out_b",
    "L0_attn_b_proj", "L0_attn_b_gamma", "L0_attn_b_beta",
    "L0_ffn_w_gate", "L0_ffn_w_up", "L0_ffn_w_down",
]

CVALS = [1e-5, 1e-6, MAGIC, -MAGIC, 0.5, -0.5, 1.0 / 127.0, 127.0,
         1.0, 0.0, 2.0, 0.25, 0.125, 3.0, 4.0, 5.0]


def _rope_tables():
    freqs = 1.0 / (10000.0 ** (np.arange(0, D, 2, dtype=np.float64) / D))
    t = np.arange(T, dtype=np.float64)
    ang = t[:, None] * freqs[None, :]
    cos, sin = np.cos(ang), np.sin(ang)
    C2 = np.empty((T, D), dtype=np.float64)
    S2 = np.empty((T, D), dtype=np.float64)
    C2[:, 0::2] = cos
    C2[:, 1::2] = cos
    S2[:, 0::2] = -sin
    S2[:, 1::2] = sin
    C2 = C2.reshape(NT, 128, D).transpose(1, 0, 2).reshape(128, NT * D)
    S2 = S2.reshape(NT, 128, D).transpose(1, 0, 2).reshape(128, NT * D)
    return C2.astype(np.float32), S2.astype(np.float32)


def const_inputs():
    C2np, S2np = _rope_tables()
    return {
        "scr_tg": np.zeros((E, FI), dtype=ml_dtypes.bfloat16),
        "scr_tu": np.zeros((E, FI), dtype=ml_dtypes.bfloat16),
        "scr_td": np.zeros((FI, E), dtype=ml_dtypes.bfloat16),
        "scr_h": np.zeros((T, FI), dtype=np.float32),
        "cst_ident": np.eye(128, dtype=np.float32),
        "cst_identb": np.eye(128, dtype=ml_dtypes.bfloat16),
        "cst_c2": C2np,
        "cst_s2": S2np,
        "cst_mask": np.tril(np.ones((128, 128), dtype=np.float32)).T.copy(),
        "cst_cvals": np.asarray([CVALS], dtype=np.float32),
    }


class Sem:
    def __init__(self, h):
        self.h = h
        self.n = 0


class Prog:

    ENGS = ("sync", "tensor", "vector", "scalar", "gpsimd")
    ESEM = {"tensor": "te", "vector": "ve", "scalar": "ac", "gpsimd": "gp",
            "sync": "sy"}

    def __init__(self, nc, ctx):
        self.nc = nc
        self.q = {e: [] for e in self.ENGS}
        names = ["te", "ve", "ac", "gp", "sy", "st"]
        names += [f"ld{i}" for i in range(4)] + [f"gld{i}" for i in range(4)]
        self.sem = {n: Sem(ctx.enter_context(nc.semaphore(f"s_{n}")))
                    for n in names}
        self.rr = {"sync": 0, "gpsimd": 0}

    def op(self, eng, fn):
        s = self.sem[self.ESEM[eng]]
        s.n += 1
        v = s.n

        def f(e, fn=fn, s=s):
            fn(e).then_inc(s.h, 1)
        self.q[eng].append(f)
        return (self.ESEM[eng], v)

    def dma(self, eng, out, in_, sname=None, slow=False):
        if sname is None:
            base = {"sync": "ld", "gpsimd": "gld"}[eng]
            i = self.rr[eng]
            self.rr[eng] = (i + 1) % 4
            sname = f"{base}{i}"
        s = self.sem[sname]
        if s.n:
            self.wait(eng, (sname, s.n))
        s.n += 16
        v = s.n

        def f(e, out=out, in_=in_, s=s, slow=slow):
            e.dma_start(out=out, in_=in_,
                        allow_slow_non_contiguous=slow).then_inc(s.h, 16)
        self.q[eng].append(f)
        return (sname, v)

    def wait(self, eng, tk):
        if tk is None:
            return
        if isinstance(tk, (list, tuple)) and tk and isinstance(tk[0], (list, tuple, type(None))):
            for t in tk:
                self.wait(eng, t)
            return
        if isinstance(tk, list):
            for t in tk:
                self.wait(eng, t)
            return
        sname, v = tk
        s = self.sem[sname]
        assert v <= s.n, f"wait on future count {sname} {v} > {s.n}"

        def f(e, s=s, v=v):
            e.wait_ge(s.h, v)
        self.q[eng].append(f)

    def run(self, block):
        for eng in self.ENGS:
            ops = self.q[eng]

            def body(e, ops=ops):
                for fn in ops:
                    fn(e)

            getattr(block, eng)(body)


def build_program(stages=None, debug=None):
    if stages is None:
        stages = STAGES
    if debug is None:
        debug = DEBUG
    nc = bass.Bass("TRN2", target_bir_lowering=False)

    x_d = nc.dram_tensor("x", [T, E], F32, kind="ExternalInput").ap()
    w_d = {}
    for k in WEIGHT_KEYS:
        shp = {
            "L0_attn_b_q_b": [E], "L0_attn_b_k_b": [E], "L0_attn_b_v_b": [E],
            "L0_attn_b_out_b": [E], "L0_attn_b_gamma": [E], "L0_attn_b_beta": [E],
            "L0_attn_b_proj": [M, D],
            "L0_ffn_w_gate": [FI, E], "L0_ffn_w_up": [FI, E],
            "L0_ffn_w_down": [E, FI],
        }.get(k, [E, E])
        w_d[k] = nc.dram_tensor(k, shp, F32, kind="ExternalInput").ap()
    y_d = nc.dram_tensor("y", [T, E], F32, kind="ExternalOutput").ap()
    tern_g = nc.dram_tensor("scr_tg", [E, FI], BF16, kind="ExternalInput").ap()
    tern_u = nc.dram_tensor("scr_tu", [E, FI], BF16, kind="ExternalInput").ap()
    tern_d = nc.dram_tensor("scr_td", [FI, E], BF16, kind="ExternalInput").ap()
    h_scr = nc.dram_tensor("scr_h", [T, FI], F32, kind="ExternalInput").ap()
    ident_c = nc.dram_tensor("cst_ident", [128, 128], F32, kind="ExternalInput").ap()
    identb_c = nc.dram_tensor("cst_identb", [128, 128], BF16, kind="ExternalInput").ap()
    c2_c = nc.dram_tensor("cst_c2", [128, NT * D], F32, kind="ExternalInput").ap()
    s2_c = nc.dram_tensor("cst_s2", [128, NT * D], F32, kind="ExternalInput").ap()
    mask_c = nc.dram_tensor("cst_mask", [128, 128], F32, kind="ExternalInput").ap()
    cvals_c = nc.dram_tensor("cst_cvals", [1, 16], F32, kind="ExternalInput").ap()

    ctx = contextlib.ExitStack()

    def sbuf(name, shape, dtype):
        return ctx.enter_context(nc.sbuf_tensor(name, shape, dtype))

    X = sbuf("X", [128, NT * E], F32)
    A16 = sbuf("A16", [128, 50176], BF16)
    A32 = sbuf("A32", [128, 9280], F32)
    CONST = sbuf("CONST", [128, 2176], F32)
    IDENTB = sbuf("IDENTB", [128, 128], BF16)
    STATS = sbuf("STATS", [128, 64], F32)

    ident = CONST[:, 0:128]
    c2 = CONST[:, 128:1152]
    s2 = CONST[:, 1152:2176]

    LNH = A16[:, 0:16384]
    LNL = A16[:, 16384:32768]
    SLAB = [(A16[:, 32768:36864], A16[:, 36864:40960]),
            (A16[:, 40960:45056], A16[:, 45056:49152])]
    afm_h = A16[:, 49152:49664]
    afm_l = A16[:, 49664:50176]

    W32 = A32[:, 0:4096]
    ln_tm = A32[:, 4096:5120]
    vkbd = A32[:, 5120:6160]
    vpad = A32[:, 6160:6680]
    krr = A32[:, 6680:7192]
    qrr = A32[:, 7192:7704]
    qfm = A32[:, 7704:8216]
    attn_tm = A32[:, 8216:8728]
    t1buf = A32[:, 8728:9240]
    dens = A32[:, 9240:9256]
    sq2 = A32[:, 6680:7704]

    mu = STATS[:, 0:16]
    rstd = STATS[:, 16:32]
    nmr = STATS[:, 32:48]
    CONSTV = STATS[:, 48:64]

    def cv(val):
        return CONSTV[:, CVALS.index(val):CVALS.index(val) + 1]

    PS = [ctx.enter_context(nc.psum_tensor(f"pb{i}", [128, 512], F32))
          for i in range(8)]

    p = Prog(nc, ctx)
    op, dma, wait = p.op, p.dma, p.wait

    boot = []
    boot.append(dma("sync", CONST[:, 0:128], ident_c))
    boot.append(dma("sync", CONST[:, 128:1152], c2_c))
    boot.append(dma("sync", CONST[:, 1152:2176], s2_c))
    boot.append(dma("sync", IDENTB[:], identb_c))
    boot.append(dma("sync", CONSTV[:, :], cvals_c[0:1, :].partition_broadcast(128)))
    xin = x_d.rearrange("(a p) e -> p a e", p=128)
    boot.append(dma("sync", X[:, :].rearrange("p (a e) -> p a e", e=E), xin))
    t_x = boot

    def ln_stats(eps, src=None, deps=None):
        if src is None:
            src = X
        t_sq = {0: None, 1: None}
        t_red = {0: None, 1: None}
        sqb = [ln_tm[:, 0:1024], sq2]
        t_parts = {0: None, 1: None}
        for tau in range(NT):
            sl = src[:, tau * E:(tau + 1) * E]
            b = sqb[tau % 2]
            wait("vector", t_red[tau % 2])
            if tau < 2:
                wait("vector", deps)
            t_sq[tau % 2] = op("vector", lambda v, sl=sl, b=b: v.tensor_mul(b, sl, sl))
            b3 = b.rearrange("p (g e) -> p g e", e=128)
            s3 = sl.rearrange("p (g e) -> p g e", e=128)
            wait("vector", [t_sq[tau % 2], t_parts[0], t_parts[1]])
            tr1 = op("vector", lambda v, s3=s3: v.reduce_sum(
                t1buf[:, 0:8].rearrange("p (g o) -> p g o", o=1), s3, axis=AX.X))
            tr2 = op("vector", lambda v, b3=b3: v.reduce_sum(
                t1buf[:, 8:16].rearrange("p (g o) -> p g o", o=1), b3, axis=AX.X))
            t_parts[0], t_parts[1] = tr1, tr2
            wait("vector", [tr1, tr2])
            op("vector", lambda v, tau=tau: v.reduce_sum(
                mu[:, tau:tau + 1], t1buf[:, 0:8], axis=AX.X))
            t_red[tau % 2] = op("vector", lambda v, tau=tau: v.reduce_sum(
                rstd[:, tau:tau + 1], t1buf[:, 8:16], axis=AX.X))
        wait("vector", [t_red[0], t_red[1]])
        t1 = op("vector", lambda v: v.tensor_scalar_mul(mu[:, :], mu[:, :], 1.0 / E))
        t2 = op("vector", lambda v: v.tensor_scalar_mul(rstd[:, :], rstd[:, :], 1.0 / E))
        wait("vector", t1)
        t3 = op("vector", lambda v: v.tensor_mul(nmr[:, :], mu[:, :], mu[:, :]))
        wait("vector", [t2, t3])
        t4 = op("vector", lambda v: v.tensor_tensor(rstd[:, :], rstd[:, :], nmr[:, :], OP.subtract))
        wait("vector", t4)
        t5 = op("vector", lambda v, eps=eps: v.tensor_scalar_add(rstd[:, :], rstd[:, :], float(eps)))
        wait("scalar", t5)
        t6 = op("scalar", lambda s: s.activation(rstd[:, :], rstd[:, :], AF.Ln))
        wait("vector", t6)
        t7 = op("vector", lambda v: v.tensor_scalar_mul(rstd[:, :], rstd[:, :], -0.5))
        wait("scalar", t7)
        t8 = op("scalar", lambda s: s.activation(rstd[:, :], rstd[:, :], AF.Exp))
        wait("vector", t8)
        t9 = op("vector", lambda v: v.tensor_mul(nmr[:, :], mu[:, :], rstd[:, :]))
        wait("vector", t9)
        t10 = op("vector", lambda v: v.tensor_scalar_mul(nmr[:, :], nmr[:, :], -1.0))
        return t10

    def build_ln_fm(t_stats, t_scratch_free):
        lnh3 = LNH[:, :].rearrange("p (c t) -> p c t", t=2048)
        lnl3 = LNL[:, :].rearrange("p (c t) -> p c t", t=2048)
        prev_ft = {0: [], 1: []}
        prev_fh = {0: None, 1: None}
        prev_fl = {0: None, 1: None}
        outs = []
        for tau in range(NT):
            wait("scalar", t_stats if tau == 0 else None)
            wait("scalar", t_scratch_free if tau == 0 else None)
            wait("scalar", prev_ft[0])
            wait("scalar", prev_ft[1])
            t_ln = op("scalar", lambda s, tau=tau: s.activation(
                ln_tm[:, :], X[:, tau * E:(tau + 1) * E], AF.Identity,
                scale=rstd[:, tau:tau + 1], bias=nmr[:, tau:tau + 1]))
            for half in range(2):
                bank = PS[6 + half]
                ft = []
                for j in range(4):
                    if j == 0:
                        wait("tensor", t_ln)
                        wait("tensor", prev_fh[half])
                        wait("tensor", prev_fl[half])
                    ft.append(op("tensor", lambda t, half=half, bank=bank, j=j: t.matmul(
                        bank[:, j * 128:(j + 1) * 128],
                        ln_tm[:, (half * 4 + j) * 128:(half * 4 + j + 1) * 128],
                        ident, start=(j == 0), stop=(j == 3), is_transpose=True)))
                prev_ft[half] = ft
                dsth = lnh3[:, half * 4:half * 4 + 4, tau * 128:tau * 128 + 128]
                dstl = lnl3[:, half * 4:half * 4 + 4, tau * 128:tau * 128 + 128]
                src3 = bank[:, 0:512].rearrange("p (c t) -> p c t", t=128)
                wait("scalar", ft)
                t_fh = op("scalar", lambda s, dsth=dsth, src3=src3: s.activation(
                    dsth, src3, AF.Copy))
                wait("vector", ft)
                wait("vector", t_fh)
                t_fl = op("vector", lambda v, dstl=dstl, src3=src3, dsth=dsth:
                          v.tensor_tensor(dstl, src3, dsth, OP.subtract))
                prev_fh[half] = t_fh
                prev_fl[half] = t_fl
                if tau == NT - 1:
                    outs += [t_fh, t_fl]
        return outs

    def load_slab(slot, wkey, rows, cols, t_w32_free, t_slab_free):
        whi, wlo = SLAB[slot]
        r0, r1 = rows
        c0, c1 = cols
        nw = c1 - c0
        nch = (r1 - r0) // 128
        src = w_d[wkey][r0:r1, c0:c1].rearrange("(c p) n -> p c n", p=128)
        wait("sync", t_w32_free)
        t32 = dma("sync", W32[:, 0:nch * nw].rearrange("p (c n) -> p c n", n=nw), src)
        wait("gpsimd", t_slab_free)
        thi = dma("gpsimd", whi[:, 0:nch * nw].rearrange("p (c n) -> p c n", n=nw), src)
        wait("vector", [t32, thi])
        wait("vector", t_slab_free)
        tlo = op("vector", lambda v, n=nch * nw: v.tensor_tensor(
            wlo[:, 0:n], W32[:, 0:n], whi[:, 0:n], OP.subtract))
        return tlo

    def proj_tm(psum, whi, wlo, tau, n0, nw):
        def f(t, psum=psum, whi=whi, wlo=wlo, tau=tau, n0=n0, nw=nw):
            r = None
            for c in range(NF):
                lh = LNH[:, c * 2048 + tau * 128: c * 2048 + tau * 128 + 128]
                ll = LNL[:, c * 2048 + tau * 128: c * 2048 + tau * 128 + 128]
                wh = whi[:, c * 512 + n0: c * 512 + n0 + nw]
                wl = wlo[:, c * 512 + n0: c * 512 + n0 + nw]
                t.matmul(psum[:, 0:nw], lh, wh, start=(c == 0), stop=False)
                t.matmul(psum[:, 0:nw], lh, wl, start=False, stop=False)
                r = t.matmul(psum[:, 0:nw], ll, wh, start=False, stop=(c == NF - 1))
            return r
        return op("tensor", f)

    def rope_relu(dst, psum, tau, deps_evac, deps_t2, t2):
        wait("scalar", deps_evac)
        t_ev = op("scalar", lambda s: s.activation(dst[:, :], psum[:, 0:512], AF.Copy))
        c2b = c2[:, tau * 64:(tau + 1) * 64].unsqueeze(1).broadcast_to([128, 8, 64])
        s2b = s2[:, tau * 64:(tau + 1) * 64].rearrange(
            "p (q two) -> p q two", two=2).unsqueeze(1).broadcast_to([128, 8, 32, 2])
        d4 = dst[:, :].rearrange("p (h d) -> p h d", d=64)
        dswap = dst[:, :].rearrange("p (h q two) -> p h q two", h=8, two=2)[:, :, :, ::-1]
        t4 = t1buf[:, :].rearrange("p (h d) -> p h d", d=64)
        t24 = t2[:, :].rearrange("p (h q two) -> p h q two", h=8, two=2)
        wait("vector", t_ev)
        t_f1 = op("vector", lambda v: v.tensor_tensor(t4, d4, c2b, OP.mult))
        wait("vector", deps_t2)
        t_f2 = op("vector", lambda v: v.tensor_tensor(t24, dswap, s2b, OP.mult))
        wait("vector", [t_f1, t_f2])
        t_f3 = op("vector", lambda v: v.tensor_add(dst[:, :], t1buf[:, :], t2[:, :]))
        wait("vector", t_f3)
        t_f4 = op("vector", lambda v: v.tensor_scalar_max(dst[:, :], dst[:, :], 0.0))
        return t_ev, t_f4

    def stage1():
        t_stats = ln_stats(1e-5, deps=t_x)
        t_lnfm = build_ln_fm(t_stats, t_stats)

        pv = {}
        t_slab_free = None
        t_vkbd = []
        t_fv = None
        t_kev = None
        t_krr = None
        fvk = []
        for half in range(2):
            t_wk = load_slab(0, "L0_attn_a_Wk", (0, E), (half * 512, (half + 1) * 512),
                             None if half == 0 else [t_kev],
                             None if half == 0 else [t_kev, fvk[-1]])
            t_wv = load_slab(1, "L0_attn_a_Wv", (0, E), (half * 512, (half + 1) * 512),
                             t_wk, None if half == 0 else [t_fv, fvk[-1]])
            wait("vector", fvk[-1] if fvk else None)
            wait("vector", t_fv)
            t_ones = op("vector", lambda v: v.memset(
                vpad[:, :].rearrange("p (q s) -> p q s", s=130)[:, :, 64:130:65], 1.0))
            for tau in range(NT):
                if tau == 0:
                    wait("tensor", t_wv)
                    if half == 0:
                        wait("tensor", t_lnfm)
                wait("tensor", t_fv)
                t_pv = proj_tm(PS[0], SLAB[1][0], SLAB[1][1], tau, 0, 512)
                wait("scalar", t_pv)
                wait("scalar", fvk[-1] if fvk else None)
                wait("scalar", t_ones if tau == 0 else None)
                t_fv = op("scalar", lambda s: s.activation(
                    vpad[:, :].rearrange("p (q s d) -> p q s d", s=2, d=65)[:, :, :, 0:64],
                    PS[0][:, 0:512].rearrange("p (q d) -> p q d", d=128).rearrange(
                        "p q (s d) -> p q s d", d=64),
                    AF.Copy))
                if tau == 0:
                    wait("tensor", t_wk)
                wait("tensor", t_kev)
                t_pk = proj_tm(PS[1], SLAB[0][0], SLAB[0][1], tau, 0, 512)
                t_kev, t_krr = rope_relu(
                    krr, PS[1], tau,
                    deps_evac=[t_pk] + ([fvk[-1]] if fvk else []),
                    deps_t2=None, t2=attn_tm)
                fvk = []
                for pr in range(4):
                    if pr == 0:
                        wait("tensor", [t_krr, t_fv])
                        if half == 1 and tau == 0:
                            wait("tensor", t_vkbd)
                    def fm(t, pr=pr, tau=tau):
                        out = PS[2 + pr // 2][:, (pr % 2) * 260:(pr % 2) * 260 + 130]
                        return t.matmul(out, krr[:, pr * 128:(pr + 1) * 128],
                                        vpad[:, pr * 130:(pr + 1) * 130],
                                        start=(tau == 0 and pr % 2 == 0),
                                        stop=(tau == NT - 1 and pr % 2 == 1))
                    fvk.append(op("tensor", fm))
            if half == 0:
                t_z = op("scalar", lambda s: s.memzero(vkbd[:, :]))
                t_vkbd.append(t_z)
            else:
                t_z = None
            for pr in range(4):
                wait("scalar", fvk[pr | 1])
                wait("scalar", t_z)
                def fe1(s, pr=pr, half=half):
                    src = PS[2 + pr // 2][:, (pr % 2) * 260:(pr % 2) * 260 + 130]
                    gpr = half * 4 + pr
                    return s.activation(vkbd[0:64, gpr * 130:gpr * 130 + 65],
                                        src[0:64, 0:65], AF.Copy)
                def fe2(s, pr=pr, half=half):
                    src = PS[2 + pr // 2][:, (pr % 2) * 260:(pr % 2) * 260 + 130]
                    gpr = half * 4 + pr
                    return s.activation(vkbd[64:128, gpr * 130 + 65:gpr * 130 + 130],
                                        src[64:128, 65:130], AF.Copy)
                t_vkbd.append(op("scalar", fe1))
                t_vkbd.append(op("scalar", fe2))

        if debug == "s1sweep1":
            return t_vkbd

        t_qev = None
        t_qfm = None
        ftq_last = None
        fnum = []
        fdiv = []
        fta_last = None
        t_ah = None
        t_al = None
        fop = []
        fres = []
        t_res = []
        for half in range(2):
            t_wq = load_slab(0, "L0_attn_a_Wq", (0, E), (half * 512, (half + 1) * 512),
                             [t_kev] if half == 0 else [t_qev],
                             [t_kev, fvk[-1]] if half == 0 else [t_qev, ftq_last])
            t_wo = load_slab(1, "L0_attn_a_Wo", (half * 512, (half + 1) * 512), (0, E),
                             t_wq,
                             [t_fv, fvk[-1]] if half == 0 else [fop[-1]])
            for tau in range(NT):
                if tau == 0:
                    wait("tensor", [t_wq, t_wo])
                wait("tensor", t_qev)
                t_pq = proj_tm(PS[0], SLAB[0][0], SLAB[0][1], tau, 0, 512)
                t_qev, t_qrr = rope_relu(
                    qrr, PS[0], tau,
                    deps_evac=[t_pq] + ([ftq_last] if ftq_last else []),
                    deps_t2=fta_last, t2=attn_tm)
                ftq = []
                for j in range(4):
                    if j == 0:
                        wait("tensor", t_qrr)
                        wait("tensor", t_qfm)
                    ftq.append(op("tensor", lambda t, j=j: t.matmul(
                        PS[1][:, j * 128:(j + 1) * 128], qrr[:, j * 128:(j + 1) * 128],
                        ident, start=(j == 0), stop=(j == 3), is_transpose=True)))
                ftq_last = ftq[-1]
                wait("scalar", ftq)
                wait("scalar", fnum)
                t_qfm = op("scalar", lambda s: s.activation(qfm[:, :], PS[1][:, 0:512], AF.Copy))
                fnum = []
                for pr in range(4):
                    if pr == 0:
                        wait("tensor", t_qfm)
                        if half == 0 and tau == 0:
                            wait("tensor", t_vkbd)
                        wait("tensor", fdiv)
                    def fn(t, pr=pr, half=half):
                        gpr = half * 4 + pr
                        return t.matmul(PS[2 + pr // 2][:, (pr % 2) * 130:(pr % 2) * 130 + 130],
                                        qfm[:, pr * 128:(pr + 1) * 128],
                                        vkbd[:, gpr * 130:gpr * 130 + 130],
                                        start=(pr % 2 == 0), stop=(pr % 2 == 1))
                    fnum.append(op("tensor", fn))
                wait("vector", [fnum[1], fnum[3]])
                wait("vector", fta_last)
                tds = []
                for b in range(2):
                    num3 = PS[2 + b][:, 0:260].rearrange("p (q c) -> p q c", c=130)
                    tds.append(op("vector", lambda v, b=b, num3=num3: v.tensor_scalar_add(
                        dens[:, b * 2:b * 2 + 2].unsqueeze(2), num3[:, :, 64:65], 1e-6)))
                    tds.append(op("vector", lambda v, b=b, num3=num3: v.tensor_scalar_add(
                        dens[:, 4 + b * 2:4 + b * 2 + 2].unsqueeze(2), num3[:, :, 129:130], 1e-6)))
                wait("vector", tds)
                tr1 = op("vector", lambda v: v.reciprocal(dens[:, 8:12], dens[:, 0:4]))
                tr2 = op("vector", lambda v: v.reciprocal(dens[:, 12:16], dens[:, 4:8]))
                wait("vector", [tr1, tr2])
                fdiv = []
                for b in range(2):
                    num3 = PS[2 + b][:, 0:260].rearrange("p (q c) -> p q c", c=130)
                    at4 = attn_tm[:, b * 256:(b + 1) * 256].rearrange(
                        "p (q s d) -> p q s d", s=2, d=64)
                    fdiv.append(op("vector", lambda v, num3=num3, at4=at4, b=b: v.tensor_tensor(
                        at4[:, :, 0, :], num3[:, :, 0:64],
                        dens[:, 8 + b * 2:8 + b * 2 + 2].unsqueeze(2).broadcast_to([128, 2, 64]),
                        OP.mult)))
                    fdiv.append(op("vector", lambda v, num3=num3, at4=at4, b=b: v.tensor_tensor(
                        at4[:, :, 1, :], num3[:, :, 65:129],
                        dens[:, 12 + b * 2:12 + b * 2 + 2].unsqueeze(2).broadcast_to([128, 2, 64]),
                        OP.mult)))
                fta = []
                for j in range(4):
                    if j == 0:
                        wait("tensor", fdiv)
                        wait("tensor", [t_ah, t_al])
                    fta.append(op("tensor", lambda t, j=j: t.matmul(
                        PS[6][:, j * 128:(j + 1) * 128], attn_tm[:, j * 128:(j + 1) * 128],
                        ident, start=(j == 0), stop=(j == 3), is_transpose=True)))
                fta_last = fta[-1]
                wait("scalar", fta)
                wait("scalar", fop)
                t_ah = op("scalar", lambda s: s.activation(afm_h[:, :], PS[6][:, 0:512], AF.Copy))
                wait("vector", fta)
                wait("vector", t_ah)
                wait("vector", fop)
                t_al = op("vector", lambda v: v.tensor_tensor(
                    afm_l[:, :], PS[6][:, 0:512], afm_h[:, :], OP.subtract))
                fop = []
                for n in range(2):
                    if n == 0:
                        wait("tensor", [t_ah, t_al])
                        wait("tensor", fres)
                    def fo(t, n=n):
                        whi, wlo = SLAB[1]
                        r = None
                        for cch in range(4):
                            ah = afm_h[:, cch * 128:(cch + 1) * 128]
                            al = afm_l[:, cch * 128:(cch + 1) * 128]
                            wh = whi[:, cch * 1024 + n * 512: cch * 1024 + n * 512 + 512]
                            wl = wlo[:, cch * 1024 + n * 512: cch * 1024 + n * 512 + 512]
                            t.matmul(PS[4 + n][:, 0:512], ah, wh, start=(cch == 0), stop=False)
                            t.matmul(PS[4 + n][:, 0:512], ah, wl, start=False, stop=False)
                            r = t.matmul(PS[4 + n][:, 0:512], al, wh, start=False, stop=(cch == 3))
                        return r
                    fop.append(op("tensor", fo))
                wait("vector", fop)
                fr1 = op("vector", lambda v, tau=tau: v.scalar_tensor_tensor(
                    X[:, tau * E:tau * E + 512], PS[4][:, 0:512], 1.0,
                    X[:, tau * E:tau * E + 512], OP.mult, OP.add))
                fr2 = op("vector", lambda v, tau=tau: v.scalar_tensor_tensor(
                    X[:, tau * E + 512:tau * E + 1024], PS[5][:, 0:512], 1.0,
                    X[:, tau * E + 512:tau * E + 1024], OP.mult, OP.add))
                fres = [fr1, fr2]
                t_res.extend(fres)
        return t_res

    AFm = A16[:, 0:16384]
    AINT = A16[:, 16384:17408]
    HINT = A16[:, 17408:21504]
    HFM = A16[:, 21504:25600]
    PQ = A16[:, 25600:29696]
    PST = A16[:, 29696:33792]
    GS = A16[:, 33792:37888]
    US = A16[:, 37888:41984]
    HS32 = A32[:, 5120:5632]
    SQ32 = A32[:, 5632:6144]
    ln2_tm = A32[:, 4096:5120]
    hmaxc = A32[:, 6144:6160]
    invas_a = A32[:, 6160:6176]
    invas_h = A32[:, 6176:6192]
    kapc = A32[:, 6192:6208]
    wsc = A32[:, 9256:9272]
    absacc = A32[:, 6224:6240]
    sc1 = A32[:, 6240:6256]

    def prep_weight(widx, wkey, dst, t_w32_free, t_pq_free, t_pst_free, t_ps7_free):
        wd = w_d[wkey]
        R, C = wd.shape
        nslab = R // 128
        csl = C
        t_acc = None
        t32s = []
        t_prev32 = t_w32_free
        for s in range(nslab):
            src = wd[s * 128:(s + 1) * 128, :]
            wait("sync", t_prev32)
            t32 = dma("sync", W32[:, 0:C], src)
            t32s.append(t32)
            wait("vector", t32)
            wait("vector", t_acc)
            w3 = W32[:, 0:C].rearrange("p (g e) -> p g e", e=128)
            ta = op("vector", lambda v, w3=w3: v.reduce_sum(
                t1buf[:, 0:C // 128].rearrange("p (g o) -> p g o", o=1), w3,
                axis=AX.X, apply_absolute_value=True))
            wait("vector", ta)
            tb = op("vector", lambda v: v.reduce_sum(sc1[:, 0:1], t1buf[:, 0:C // 128], axis=AX.X))
            wait("vector", tb)
            if s == 0:
                t_acc = op("vector", lambda v: v.tensor_copy(absacc[:, 0:1], sc1[:, 0:1]))
            else:
                t_acc = op("vector", lambda v: v.tensor_add(absacc[:, 0:1], absacc[:, 0:1], sc1[:, 0:1]))
            t_prev32 = ta
        wait("tensor", t_acc)
        wait("tensor", t_ps7_free)
        t_par0 = op("tensor", lambda t: t.matmul(
            PS[7][0:1, 0:1], CONSTV[:, CVALS.index(1.0):CVALS.index(1.0) + 1],
            absacc[:, 0:1], start=True, stop=True))
        wait("scalar", t_par0)
        t_par1 = op("scalar", lambda s: s.activation(absacc[0:1, 3:4], PS[7][0:1, 0:1], AF.Copy))
        wait("sync", t_par1)
        t_par2 = dma("sync", h_scr[0:1, 4000:4001], absacc[0:1, 3:4])
        wait("sync", t_par2)
        t_par = dma("sync", absacc[:, 1:2], h_scr[0:1, 4000:4001].partition_broadcast(128))
        wait("vector", t_par)
        t_m = op("vector", lambda v: v.tensor_scalar_mul(absacc[:, 2:3], absacc[:, 1:2], 1.0 / (R * C)))
        wait("vector", t_m)
        t_c = op("vector", lambda v, widx=widx: v.tensor_scalar_max(
            wsc[:, 2 * widx + 1:2 * widx + 2], absacc[:, 2:3], 1e-5))
        wait("vector", t_c)
        t_ws = op("vector", lambda v, widx=widx: v.reciprocal(
            wsc[:, 2 * widx:2 * widx + 1], wsc[:, 2 * widx + 1:2 * widx + 2]))
        PQ16 = PQ.bitcast(F16)
        t_prevq = [t_pq_free]
        t_prevs = [t_pst_free]
        t_ps7 = [t_ps7_free]
        t_dmas = []
        t_prev32b = t_prev32
        for s in range(nslab):
            src = wd[s * 128:(s + 1) * 128, :]
            wait("sync", [t_prev32b, t32s[-1]])
            t32 = dma("sync", W32[:, 0:C], src)
            wait("scalar", [t32, t_ws])
            wait("scalar", t_prevq)
            tq1 = op("scalar", lambda s_, widx=widx: s_.activation(
                PQ16[:, 0:C], W32[:, 0:C], AF.Identity,
                scale=wsc[:, 2 * widx:2 * widx + 1], bias=cv(MAGIC)))
            wait("vector", tq1)
            tq2 = op("vector", lambda v: v.tensor_scalar(
                PQ[:, 0:C], PQ16[:, 0:C], MAGIC, 1.0, OP.subtract, OP.min))
            wait("vector", tq2)
            tq3 = op("vector", lambda v: v.tensor_scalar_max(PQ[:, 0:C], PQ[:, 0:C], -1.0))
            t_prev32b = tq1
            nt_ = C // 128
            t_evs = []
            for j in range(nt_):
                if j % 4 == 0:
                    wait("tensor", tq3)
                    wait("tensor", t_prevs)
                    wait("tensor", t_ps7)
                tt = op("tensor", lambda t, j=j: t.matmul(
                    PS[7][:, :].bitcast(BF16)[:, (j % 4) * 128:(j % 4) * 128 + 128],
                    PQ[:, j * 128:(j + 1) * 128], IDENTB[:, :],
                    start=(j % 4 == 0), stop=(j % 4 == 3), is_transpose=True))
                if j % 4 == 3:
                    wait("scalar", tt)
                    wait("scalar", t_prevs)
                    tev = op("scalar", lambda s_, j=j: s_.activation(
                        PST[:, (j - 3) * 128:(j + 1) * 128],
                        PS[7][:, :].bitcast(BF16)[:, 0:512], AF.Copy))
                    t_evs.append(tev)
                    t_ps7 = [tev]
            t_prevq = t_evs[-1:]
            wait("sync", t_evs)
            if widx < 2:
                dstap = dst[0:E, s * 128:(s + 1) * 128].rearrange("(c p) i -> p c i", p=128)
                srcap = PST[:, 0:C].rearrange("p (c i) -> p c i", i=128)
                td = dma("sync", dstap, srcap)
            else:
                dstap = dst[0:FI, s * 128:(s + 1) * 128].rearrange("(c p) hh -> p c hh", p=128)
                srcap = PST[:, 0:C].rearrange("p (c hh) -> p c hh", hh=128)
                td = dma("sync", dstap, srcap)
            t_dmas.append(td)
            t_prevs = [td]
        return t_dmas, t_ws

    def stage_ffn(app, t_entry, t_prep_done):
        t_stats = ln_stats(1e-5, deps=t_entry)
        AINT16 = AINT.bitcast(F16)
        t_ln2 = None
        t_y16 = None
        t_aint = None
        t_afm = []
        prev_ev = {0: None, 1: None}
        prev_tt = {0: [], 1: []}
        for tau in range(NT):
            wait("scalar", t_stats if tau == 0 else None)
            wait("scalar", t_entry if tau == 0 else None)
            wait("scalar", prev_tt[0])
            wait("scalar", prev_tt[1])
            wait("scalar", t_prep_done["ln_tm_free"] if tau == 0 else None)
            t_ln2 = op("scalar", lambda s, tau=tau: s.activation(
                ln_tm[:, :], X[:, tau * E:(tau + 1) * E], AF.Identity,
                scale=rstd[:, tau:tau + 1], bias=nmr[:, tau:tau + 1]))
            wait("vector", t_ln2)
            wait("vector", t_y16)
            t_mx = op("vector", lambda v: v.reduce_max(
                sc1[:, 0:1], ln_tm[:, :], axis=AX.X, apply_absolute_value=True))
            wait("vector", t_mx)
            t_cl = op("vector", lambda v: v.tensor_scalar_max(sc1[:, 1:2], sc1[:, 0:1], 1e-5))
            wait("vector", t_cl)
            t_rc = op("vector", lambda v: v.reciprocal(sc1[:, 2:3], sc1[:, 1:2]))
            wait("vector", t_rc)
            t_as = op("vector", lambda v: v.tensor_scalar_mul(sc1[:, 3:4], sc1[:, 2:3], 127.0))
            t_ia = op("vector", lambda v, tau=tau: v.tensor_scalar_mul(
                invas_a[:, tau:tau + 1], sc1[:, 1:2], 1.0 / 127.0))
            wait("scalar", [t_as, t_ia])
            wait("scalar", t_aint)
            t_y16 = op("scalar", lambda s: s.activation(
                AINT16[:, :], ln_tm[:, :], AF.Identity, scale=sc1[:, 3:4], bias=cv(MAGIC)))
            wait("vector", t_y16)
            wait("vector", prev_tt[0])
            wait("vector", prev_tt[1])
            t_aint = op("vector", lambda v: v.tensor_scalar(
                AINT[:, :], AINT16[:, :], MAGIC, None, OP.subtract))
            for half in range(2):
                tt4 = []
                for j in range(4):
                    cc = half * 4 + j
                    if j == 0:
                        wait("tensor", t_aint)
                        wait("tensor", prev_ev[half])
                    tt4.append(op("tensor", lambda t, cc=cc, j=j, half=half: t.matmul(
                        PS[6 + half][:, :].bitcast(BF16)[:, j * 128:(j + 1) * 128],
                        AINT[:, cc * 128:(cc + 1) * 128], IDENTB[:, :],
                        start=(j == 0), stop=(j == 3), is_transpose=True)))
                prev_tt[half] = tt4
                dsts = AFm[:, :].rearrange("p (c t) -> p c t", t=2048)[
                    :, half * 4:half * 4 + 4, tau * 128:tau * 128 + 128]
                wait("scalar", tt4)
                if tau == 0:
                    wait("scalar", t_prep_done["afm_free"])
                tev = op("scalar", lambda s, dsts=dsts, half=half: s.activation(
                    dsts, PS[6 + half][:, :].bitcast(BF16)[:, 0:512].rearrange(
                        "p (c t) -> p c t", t=128), AF.Copy))
                prev_ev[half] = tev
                if tau == NT - 1:
                    t_afm.append(tev)
        t_gs = None
        t_us = None
        t_hmax = {}
        t_hdma = {}
        t_gev = None
        t_hst = None
        t_relu = None
        t_sq = None
        t_h = None
        fmm_g = None
        fmm_u = None
        for isl in range(8):
            gsrc = tern_g[0:E, isl * 512:(isl + 1) * 512].rearrange("(c p) i -> p c i", p=128)
            usrc = tern_u[0:E, isl * 512:(isl + 1) * 512].rearrange("(c p) i -> p c i", p=128)
            wait("sync", t_prep_done["g_dmas"] if isl == 0 and app == 0 else None)
            wait("sync", t_entry if isl == 0 else None)
            wait("sync", [fmm_g] if isl > 0 else None)
            t_gs = dma("sync", GS[:, :].rearrange("p (c i) -> p c i", i=512), gsrc)
            wait("sync", t_prep_done["u_dmas"] if isl == 0 and app == 0 else None)
            wait("sync", [fmm_u] if isl > 0 else None)
            t_us = dma("sync", US[:, :].rearrange("p (c i) -> p c i", i=512), usrc)
            for tau in range(NT):
                if tau == 0:
                    wait("tensor", [t_gs, t_us])
                    if isl == 0:
                        wait("tensor", t_afm)
                wait("tensor", t_relu)
                def fg(t, tau=tau):
                    r = None
                    for c in range(NF):
                        r = t.matmul(PS[0][:, 0:512],
                                     AFm[:, c * 2048 + tau * 128:c * 2048 + tau * 128 + 128],
                                     GS[:, c * 512:(c + 1) * 512],
                                     start=(c == 0), stop=(c == NF - 1))
                    return r
                fmm_g = op("tensor", fg)
                wait("tensor", t_h)
                def fu(t, tau=tau):
                    r = None
                    for c in range(NF):
                        r = t.matmul(PS[1][:, 0:512],
                                     AFm[:, c * 2048 + tau * 128:c * 2048 + tau * 128 + 128],
                                     US[:, c * 512:(c + 1) * 512],
                                     start=(c == 0), stop=(c == NF - 1))
                    return r
                fmm_u = op("tensor", fu)
                wait("vector", fmm_g)
                wait("vector", t_sq)
                wait("vector", t_hst)
                t_relu = op("vector", lambda v: v.tensor_scalar_max(HS32[:, :], PS[0][:, 0:512], 0.0))
                wait("scalar", t_relu)
                wait("scalar", t_h)
                t_sq = op("scalar", lambda s: s.activation(SQ32[:, :], HS32[:, :], AF.Square))
                wait("vector", [t_sq, fmm_u])
                wait("vector", t_hst)
                t_h = op("vector", lambda v: v.tensor_tensor(HS32[:, :], SQ32[:, :], PS[1][:, 0:512], OP.mult))
                wait("vector", t_h)
                t_m1 = op("vector", lambda v: v.reduce_max(
                    sc1[:, 4:5], HS32[:, :], axis=AX.X, apply_absolute_value=True))
                wait("vector", t_m1)
                if isl == 0:
                    t_hmax[tau] = op("vector", lambda v, tau=tau: v.tensor_copy(
                        hmaxc[:, tau:tau + 1], sc1[:, 4:5]))
                else:
                    wait("vector", t_hmax[tau])
                    t_hmax[tau] = op("vector", lambda v, tau=tau: v.tensor_max(
                        hmaxc[:, tau:tau + 1], hmaxc[:, tau:tau + 1], sc1[:, 4:5]))
                wait("sync", t_h)
                wait("sync", t_entry if (isl == 0 and tau == 0) else None)
                wait("sync", t_hdma.get((isl, tau)))
                dsth = h_scr[tau * 128:(tau + 1) * 128, isl * 512:(isl + 1) * 512]
                t_hst = dma("sync", dsth, HS32[:, :])
                t_hdma[(isl, tau)] = t_hst
        HINT16 = HINT.bitcast(F16)
        t_res = []
        t_td = None
        t_hld = None
        t_y16h = None
        t_hint = None
        prev_tth = []
        prev_evh = None
        fdn = None
        t_sres = None
        for n in range(2):
            tdsrc = tern_d[0:FI, n * 512:(n + 1) * 512].rearrange("(c p) hh -> p c hh", p=128)
            if n == 0:
                wait("sync", t_prep_done["d_dmas"] if app == 0 else None)
                wait("sync", [fmm_g, fmm_u])
            else:
                wait("sync", [fdn])
            t_td = dma("sync", AFm[:, :].rearrange("p (c hh) -> p c hh", hh=512), tdsrc)
            for tau in range(NT):
                wait("sync", [t_y16h] if t_y16h else None)
                hsrc = h_scr[tau * 128:(tau + 1) * 128, :]
                wait("sync", [t_hdma[(i, tau)] for i in range(8)] if n == 0 and tau == 0 else None)
                wait("sync", t_hdma[(7, tau)] if (n == 0 and tau > 0) else None)
                t_hld = dma("sync", W32[:, 0:4096], hsrc)
                wait("vector", t_hmax[tau] if n == 0 and tau == 0 else None)
                wait("vector", [t_hmax[t_] for t_ in range(NT)] if n == 0 and tau == 0 else None)
                wait("vector", t_sres)
                t_ka = op("vector", lambda v, tau=tau: v.tensor_mul(
                    sc1[:, 5:6], invas_a[:, tau:tau + 1], invas_a[:, tau:tau + 1]))
                wait("vector", t_ka)
                t_ka2 = op("vector", lambda v, tau=tau: v.tensor_mul(
                    sc1[:, 6:7], sc1[:, 5:6], invas_a[:, tau:tau + 1]))
                wait("vector", t_ka2)
                t_ka3 = op("vector", lambda v: v.tensor_mul(sc1[:, 7:8], wsc[:, 1:2], wsc[:, 1:2]))
                wait("vector", t_ka3)
                t_ka4 = op("vector", lambda v: v.tensor_mul(sc1[:, 8:9], sc1[:, 7:8], wsc[:, 3:4]))
                wait("vector", t_ka4)
                t_kap = op("vector", lambda v: v.tensor_mul(sc1[:, 9:10], sc1[:, 6:7], sc1[:, 8:9]))
                wait("vector", t_kap)
                t_m2 = op("vector", lambda v, tau=tau: v.tensor_mul(
                    sc1[:, 10:11], sc1[:, 9:10], hmaxc[:, tau:tau + 1]))
                wait("vector", t_m2)
                t_c2 = op("vector", lambda v: v.tensor_scalar_max(sc1[:, 11:12], sc1[:, 10:11], 1e-5))
                wait("vector", t_c2)
                t_r2 = op("vector", lambda v: v.reciprocal(sc1[:, 12:13], sc1[:, 11:12]))
                wait("vector", t_r2)
                t_s1 = op("vector", lambda v: v.tensor_mul(sc1[:, 13:14], sc1[:, 9:10], sc1[:, 12:13]))
                wait("vector", t_s1)
                t_se = op("vector", lambda v: v.tensor_scalar_mul(sc1[:, 13:14], sc1[:, 13:14], 127.0))
                t_ih = op("vector", lambda v: v.tensor_scalar_mul(sc1[:, 14:15], sc1[:, 11:12], 1.0 / 127.0))
                wait("scalar", [t_hld, t_se])
                wait("scalar", t_hint)
                t_y16h = op("scalar", lambda s: s.activation(
                    HINT16[:, :], W32[:, 0:4096], AF.Identity, scale=sc1[:, 13:14], bias=cv(MAGIC)))
                wait("vector", t_y16h)
                wait("vector", prev_tth)
                t_hint = op("vector", lambda v: v.tensor_scalar(
                    HINT[:, :], HINT16[:, :], MAGIC, None, OP.subtract))
                tth = []
                evh = []
                for j8 in range(8):
                    bank = PS[6 + (j8 % 2)]
                    for j in range(4):
                        jj = j8 * 4 + j
                        if j == 0:
                            wait("tensor", t_hint if j8 == 0 else None)
                            wait("tensor", evh[-2] if len(evh) >= 2 else
                                 (prev_evh if j8 < 2 else None))
                        tth.append(op("tensor", lambda t, jj=jj, j=j, bank=bank: t.matmul(
                            bank[:, :].bitcast(BF16)[:, j * 128:(j + 1) * 128],
                            HINT[:, jj * 128:(jj + 1) * 128], IDENTB[:, :],
                            start=(j == 0), stop=(j == 3), is_transpose=True)))
                    wait("scalar", tth[-4:])
                    wait("scalar", prev_tth[-1] if prev_tth and j8 < 1 else None)
                    evh.append(op("scalar", lambda s, j8=j8, bank=bank: s.activation(
                        HFM[:, j8 * 512:(j8 + 1) * 512],
                        bank[:, :].bitcast(BF16)[:, 0:512].rearrange(
                            "p (c t) -> p c t", t=128), AF.Copy)))
                prev_tth = tth
                prev_evh = evh[-1]
                wait("tensor", evh)
                if n == 0 and tau == 0:
                    pass
                wait("tensor", t_sres)
                wait("tensor", [t_td] if tau == 0 else None)
                def fd(t):
                    r = None
                    for ic in range(32):
                        r = t.matmul(PS[2][:, 0:512],
                                     HFM[:, ic * 128:(ic + 1) * 128],
                                     AFm[:, ic * 512:(ic + 1) * 512],
                                     start=(ic == 0), stop=(ic == 31))
                    return r
                fdn = op("tensor", fd)
                wait("vector", fdn)
                wait("vector", t_ih)
                t_sd = op("vector", lambda v: v.tensor_mul(sc1[:, 15:16], sc1[:, 14:15], wsc[:, 5:6]))
                wait("vector", t_sd)
                t_sres = op("vector", lambda v, tau=tau, n=n: v.scalar_tensor_tensor(
                    X[:, tau * E + n * 512:tau * E + n * 512 + 512], PS[2][:, 0:512],
                    sc1[:, 15:16], X[:, tau * E + n * 512:tau * E + n * 512 + 512],
                    OP.mult, OP.add))
                t_res.append(t_sres)
        return t_res

    def stage3(t_entry):
        projTs = A32[:, 1024:1280]
        qb_t = A32[:, 1280:1296]
        kb_t = A32[:, 1296:1312]
        vb_t = A32[:, 1312:1440]
        ktm = A32[:, 1440:1568]
        sqb3 = A32[:, 1568:1696]
        ncr = A32[:, 1696:1712]
        scl = A32[:, 1712:1728]
        kp_tm3 = A32[:, 1728:2240]
        amask = A32[:, 2240:2496]
        ssb = A32[:, 2496:2756]
        attng = A32[:, 2756:2884]
        vp1 = A32[:, 2884:3014]
        QFM = A32[:, 3072:5120]
        KFM = A32[:, 5120:7168]
        qp_fm = A32[:, 7168:7680]
        kp_fm = A32[:, 7680:8192]
        gam = A32[:, 0:1024]
        bet = A32[:, 1024:2048]
        outbt = A32[:, 2048:3072]
        afh = afm_h[:, 0:128]
        afl = afm_l[:, 0:128]

        t_stats3 = ln_stats(1e-5, deps=t_entry)
        t_lnfm3 = build_ln_fm(t_stats3, t_entry)
        t_stash = []
        t_lnx = []
        for tau in range(NT):
            wait("sync", t_lnfm3 if tau == 0 else None)
            ts_ = dma("sync", h_scr[tau * 128:(tau + 1) * 128, 0:1024],
                      X[:, tau * E:(tau + 1) * E])
            t_stash.append(ts_)
            wait("vector", [ts_, t_stats3] if tau == 0 else ts_)
            t_lnx.append(op("vector", lambda v, tau=tau: v.tensor_scalar(
                X[:, tau * E:(tau + 1) * E], X[:, tau * E:(tau + 1) * E],
                mu[:, tau:tau + 1], rstd[:, tau:tau + 1], OP.subtract, OP.mult)))

        if debug == "s3a":
            return t_lnx
        wait("sync", t_entry)
        t_mask = dma("sync", attn_tm[:, 0:128], mask_c)
        maskt = attn_tm[:, 0:128]
        t_pj32 = dma("sync", W32[:, 0:128].rearrange("p (c d) -> p c d", d=64),
                     w_d["L0_attn_b_proj"].rearrange("(c p) d -> p c d", p=128))
        tpt = []
        for cc in range(2):
            if cc == 0:
                wait("tensor", t_pj32)
                wait("tensor", t_entry)
            tpt.append(op("tensor", lambda t, cc=cc: t.matmul(
                PS[0][0:64, cc * 128:(cc + 1) * 128],
                W32[:, cc * 64:cc * 64 + 64], ident,
                start=(cc == 0), stop=(cc == 1), is_transpose=True)))
        wait("vector", tpt)
        t_pj0 = op("vector", lambda v: v.tensor_scalar_mul(
            projTs[0:64, :], PS[0][0:64, 0:256], 0.125))
        wait("sync", t_pj0)
        t_pj = dma("sync", projTs[64:128, :], projTs[0:64, :])
        projBD = A32[:, 8728:9240]
        wait("vector", [t_pj0, t_pj])
        wait("vector", t_lnfm3)
        t_bd0 = op("vector", lambda v: v.memset(projBD[:, :], 0.0))
        wait("vector", t_bd0)
        t_pjbd = [op("vector", lambda v: v.tensor_copy(projBD[0:64, 0:256], projTs[0:64, :])),
                  op("vector", lambda v: v.tensor_copy(projBD[64:128, 256:512], projTs[64:128, :]))]
        t_qb = dma("sync", qb_t[:, 0:8], w_d["L0_attn_b_q_b"].rearrange("(c p) -> p c", p=128), slow=True)
        t_kb = dma("sync", kb_t[:, 0:8], w_d["L0_attn_b_k_b"].rearrange("(c p) -> p c", p=128), slow=True)
        if debug == "s3b":
            wait("vector", [t_pj, t_qb, t_kb, t_mask])
            return t_lnx + [op("vector", lambda v: v.tensor_copy(X[:, 0:256], projTs[:, :]))]

        t_res3 = []
        t_qf_ev = None
        t_kf_ev = None
        t_vev = None
        t_ones3 = None
        t_mapk = None
        t_kp = None
        t_kp2 = None
        t_ktev = None
        t_ncr = None
        t_sq3 = None
        t_red3 = None
        fkpt = []
        t_kpfm_ev = []
        t_qp = None
        fmq = []
        fat3 = []
        t_am = []
        fnum3 = None
        fupd = []
        t_scp = None
        t_att = None
        t_at_tr = None
        t_afh = None
        t_afl = None
        fop3 = []
        fres3 = []
        t_vb = None
        tr3g = {}
        prev_fres3 = []
        for g in range(8):
            qsl = (SLAB[0][0][:, 0:1024], SLAB[0][0][:, 1024:2048])
            ksl = (SLAB[0][0][:, 2048:3072], SLAB[0][0][:, 3072:4096])
            vsl = (SLAB[0][1][:, 0:1024], SLAB[0][1][:, 1024:2048])
            osl = (SLAB[1][0][:, 0:1024], SLAB[1][0][:, 1024:2048])
            prev_readers = [t_entry, t_pj, t_qb, t_kb] if g == 0 else [
                fmq[-1], t_kf_ev, t_vev, fop3[-1]]
            t_ws_ = []
            for (wkey, dsthl, rows, cols) in (
                    ("L0_attn_b_q_w", qsl, None, (g * 128, (g + 1) * 128)),
                    ("L0_attn_b_k_w", ksl, None, (g * 128, (g + 1) * 128)),
                    ("L0_attn_b_v_w", vsl, None, (g * 128, (g + 1) * 128)),
                    ("L0_attn_b_out_w", osl, (g * 128, (g + 1) * 128), None)):
                whi, wlo = dsthl
                if rows is None:
                    srcw = w_d[wkey][:, cols[0]:cols[1]].rearrange("(c p) n -> p c n", p=128)
                    ncols = 128
                else:
                    srcw = w_d[wkey][rows[0]:rows[1], :].rearrange("p (c n) -> p c n", n=512)
                    ncols = 512
                wait("sync", t_ws_[-1] if t_ws_ else prev_readers)
                t32w = dma("sync", W32[:, 0:1024].rearrange(
                    "p (c n) -> p c n", n=ncols), srcw)
                wait("gpsimd", prev_readers if not t_ws_ else t_ws_[-1])
                thiw = dma("gpsimd", whi.rearrange("p (c n) -> p c n", n=ncols), srcw)
                wait("vector", [t32w, thiw])
                wait("vector", prev_readers if not t_ws_ else None)
                t_ws_.append(op("vector", lambda v, wlo=wlo, whi=whi: v.tensor_tensor(
                    wlo, W32[:, 0:1024], whi, OP.subtract)))
            t_wq3, t_wk3, t_wv3, t_wo3 = t_ws_
            wait("sync", t_vev)
            t_vb = dma("sync", vb_t[:, :], w_d["L0_attn_b_v_b"][g * 128:(g + 1) * 128]
                       .unsqueeze(0).partition_broadcast(128))
            wait("vector", fupd)
            t_ones3 = op("vector", lambda v: v.memset(
                vp1[:, :].rearrange("p (s d) -> p s d", d=65)[:, :, 64:65], 1.0))

            for (dst, slhl, bias_t, tk) in ((QFM, qsl, qb_t, "q"), (KFM, ksl, kb_t, "k")):
                for tt4 in range(4):
                    if tt4 == 0 and g == 0 and tk == "q":
                        wait("tensor", t_lnfm3)
                    wait("tensor", t_wq3 if tk == "q" else t_wk3)
                    wait("tensor", [t_qf_ev, t_kf_ev])
                    def fpq(t, slhl=slhl, tt4=tt4):
                        whi, wlo = slhl
                        r = None
                        for c in range(NF):
                            lh = LNH[:, c * 2048 + tt4 * 512: c * 2048 + (tt4 + 1) * 512]
                            ll = LNL[:, c * 2048 + tt4 * 512: c * 2048 + (tt4 + 1) * 512]
                            wh = whi[:, c * 128:(c + 1) * 128]
                            wl = wlo[:, c * 128:(c + 1) * 128]
                            t.matmul(PS[0][:, 0:512], wh, lh, start=(c == 0), stop=False)
                            t.matmul(PS[0][:, 0:512], wh, ll, start=False, stop=False)
                            r = t.matmul(PS[0][:, 0:512], wl, lh, start=False, stop=(c == NF - 1))
                        return r
                    tpp = op("tensor", fpq)
                    wait("scalar", tpp)
                    wait("scalar", [t_qb, t_kb] if g == 0 and tt4 == 0 and tk == "q" else None)
                    tev = op("scalar", lambda s, dst=dst, tt4=tt4, bias_t=bias_t, g=g:
                             s.activation(dst[:, tt4 * 512:(tt4 + 1) * 512], PS[0][:, 0:512],
                                          AF.Identity, bias=bias_t[:, g:g + 1]))
                    if tk == "q":
                        t_qf_ev = tev
                    else:
                        t_kf_ev = tev

            if debug == "s3c" and g == 0:
                wait("vector", [t_qf_ev, t_kf_ev])
                return t_lnx + [op("vector", lambda v: v.tensor_copy(X[:, 0:256], QFM[:, 0:256]))]
            for tau in range(NT):
                wait("tensor", [t_vev, t_ktev, t_kp, t_kp2, t_kpfm_ev, t_qp,
                                t_scp, t_att, t_afh, t_afl])
                wait("tensor", t_am)
                wait("tensor", prev_fres3)
                wait("tensor", [t_wv3, t_vev])
                def fpv(t, tau=tau):
                    whi, wlo = vsl
                    r = None
                    for c in range(NF):
                        lh = LNH[:, c * 2048 + tau * 128: c * 2048 + tau * 128 + 128]
                        ll = LNL[:, c * 2048 + tau * 128: c * 2048 + tau * 128 + 128]
                        wh = whi[:, c * 128:(c + 1) * 128]
                        wl = wlo[:, c * 128:(c + 1) * 128]
                        t.matmul(PS[1][:, 0:128], lh, wh, start=(c == 0), stop=False)
                        t.matmul(PS[1][:, 0:128], lh, wl, start=False, stop=False)
                        r = t.matmul(PS[1][:, 0:128], ll, wh, start=False, stop=(c == NF - 1))
                    return r
                fv3 = op("tensor", fpv)
                wait("vector", [fv3, t_vb])
                wait("vector", t_ones3 if tau == 0 else None)
                wait("vector", fnum3)
                wait("vector", fupd)
                t_vev = op("vector", lambda v: v.tensor_add(
                    vp1[:, :].rearrange("p (s d) -> p s d", d=65)[:, :, 0:64],
                    PS[1][:, 0:128].rearrange("p (s d) -> p s d", d=64), vb_t[:, :].rearrange(
                        "p (s d) -> p s d", d=64)))
                wait("tensor", t_kf_ev if tau == 0 else None)
                wait("tensor", [t_ktev] if t_ktev else None)
                wait("tensor", prev_fres3)
                wait("tensor", t_kpfm_ev)
                t_ktr = op("tensor", lambda t, tau=tau: t.matmul(
                    PS[2][:, 0:128], KFM[:, tau * 128:(tau + 1) * 128], ident,
                    start=True, stop=True, is_transpose=True))
                wait("vector", t_ktr)
                wait("vector", t_red3)
                t_ktev = op("vector", lambda v: v.tensor_copy(ktm[:, :], PS[2][:, 0:128]))
                wait("vector", t_ktev)
                t_sq3 = op("vector", lambda v: v.tensor_mul(sqb3[:, :], ktm[:, :], ktm[:, :]))
                wait("vector", t_sq3)
                t_red3 = op("vector", lambda v: v.reduce_sum(
                    scl[:, 0:2].rearrange("p (s o) -> p s o", o=1),
                    sqb3[:, :].rearrange("p (s d) -> p s d", d=64), axis=AX.X))
                wait("vector", t_red3)
                t_ncr = op("vector", lambda v: v.tensor_scalar_mul(
                    ncr[:, 0:2], scl[:, 0:2], -1.0 / 128.0))
                if debug == "s3d" and g == 0 and tau == 0:
                    wait("vector", [t_vev, t_ncr])
                    return t_lnx + [op("vector", lambda v: v.tensor_copy(X[:, 0:130], vp1[:, :]))]
                wait("tensor", [t_ktev])
                wait("tensor", [t_kp, t_kp2] if t_kp else None)
                wait("tensor", ([t_pj, t_qf_ev] + t_pjbd) if tau == 0 else None)
                def fmk(t, tau=tau):
                    return t.matmul(PS[3][:, 0:512],
                                    KFM[:, tau * 128:(tau + 1) * 128],
                                    projBD[:, :], start=True, stop=True)
                t_mapk = op("tensor", fmk)
                if debug == "s3d3" and g == 0 and tau == 0:
                    wait("vector", [t_mapk, t_ncr])
                    return t_lnx + [op("vector", lambda v: v.tensor_copy(X[:, 0:512], PS[3][:, 0:512]))]
                wait("scalar", [t_mapk, t_ncr])
                wait("scalar", fkpt)
                t_kp = op("scalar", lambda s: s.activation(
                    kp_tm3[:, :].rearrange("p (s m) -> p s m", m=256),
                    PS[3][:, 0:512].rearrange("p (s m) -> p s m", m=256),
                    AF.Exp, bias=ncr[:, 0:2].rearrange("p s -> p s 1") if False else None) if False else
                    s.activation(kp_tm3[:, 0:256], PS[3][:, 0:256], AF.Exp, bias=ncr[:, 0:1]))
                t_kp2 = op("scalar", lambda s: s.activation(
                    kp_tm3[:, 256:512], PS[3][:, 256:512], AF.Exp, bias=ncr[:, 1:2]))
                if debug == "s3d4" and g == 0 and tau == 0:
                    wait("vector", [t_kp, t_kp2])
                    return t_lnx + [op("vector", lambda v: v.tensor_copy(X[:, 0:512], kp_tm3[:, :]))]
                fkpt = []
                for j in range(4):
                    if j == 0:
                        wait("tensor", [t_kp, t_kp2])
                        wait("tensor", t_kpfm_ev)
                    fkpt.append(op("tensor", lambda t, j=j: t.matmul(
                        PS[2][:, 128 + j * 128:128 + (j + 1) * 128] if False else
                        PS[2][:, j * 128:(j + 1) * 128],
                        kp_tm3[:, j * 128:(j + 1) * 128], ident,
                        start=(j == 0), stop=(j == 3), is_transpose=True)))
                wait("scalar", fkpt)
                wait("scalar", fat3)
                wait("scalar", fupd)
                t_kpfm_ev = op("scalar", lambda s: s.activation(
                    kp_fm[:, :], PS[2][:, 0:512], AF.Copy))
                if debug == "s3d2" and g == 0 and tau == 0:
                    wait("vector", [t_ncr, t_ktev])
                    return t_lnx + [op("vector", lambda v: v.tensor_copy(X[:, 0:128], ktm[:, :]))]
                if debug == "s3e" and g == 0 and tau == 0:
                    wait("vector", [t_kpfm_ev])
                    return t_lnx + [op("vector", lambda v: v.tensor_copy(X[:, 0:512], kp_fm[:, :]))]
                fmq = []
                for ms in range(4):
                    if ms == 0:
                        wait("tensor", t_qp if t_qp else None)
                    def fq3(t, ms=ms, tau=tau):
                        return t.matmul(PS[4][:, ms * 128:(ms + 1) * 128],
                                        projBD[:, ms * 128:(ms + 1) * 128],
                                        QFM[:, tau * 128:(tau + 1) * 128],
                                        start=(ms == 0), stop=(ms == 3))
                    fmq.append(op("tensor", fq3))
                wait("scalar", fmq)
                wait("scalar", fat3)
                wait("scalar", fnum3)
                t_qp = op("scalar", lambda s: s.activation(qp_fm[:, :], PS[4][:, 0:512], AF.Exp))
                fat3 = []
                for h2 in range(2):
                    if h2 == 0:
                        wait("tensor", [t_qp, t_kpfm_ev])
                        wait("tensor", t_am)
                    for mc in range(2):
                        fat3.append(op("tensor", lambda t, h2=h2, mc=mc: t.matmul(
                            PS[6][:, h2 * 128:(h2 + 1) * 128],
                            kp_fm[:, (h2 * 2 + mc) * 128:(h2 * 2 + mc + 1) * 128],
                            qp_fm[:, (h2 * 2 + mc) * 128:(h2 * 2 + mc + 1) * 128],
                            start=(mc == 0 and h2 == 0), stop=(mc == 1 and h2 == 1))))
                t_am = []
                wait("vector", fat3[-1])
                wait("vector", t_mask if tau == 0 and g == 0 else None)
                wait("vector", fnum3)
                t_am.append(op("vector", lambda v: v.tensor_mul(
                    amask[:, 0:128], PS[6][:, 0:128], maskt)))
                t_am.append(op("vector", lambda v: v.tensor_mul(
                    amask[:, 128:256], PS[6][:, 128:256], maskt)))
                if debug == "s3f" and g == 0 and tau == 0:
                    wait("vector", t_am)
                    return t_lnx + [op("vector", lambda v: v.tensor_copy(X[:, 0:256], amask[:, :]))]
                wait("tensor", t_am)
                wait("tensor", [t_vev])
                wait("tensor", t_att if t_att else None)
                wait("tensor", t_scp if tau > 0 else None)
                fns = []
                for h2 in range(2):
                    def fi3(t, h2=h2, tau=tau):
                        return t.matmul(PS[7][:, h2 * 65:(h2 + 1) * 65],
                                        amask[:, h2 * 128:(h2 + 1) * 128],
                                        vp1[:, h2 * 65:(h2 + 1) * 65],
                                        start=True, stop=(tau == 0))
                    fns.append(op("tensor", fi3))
                    if tau > 0:
                        for mc in range(2):
                            def fn3(t, h2=h2, mc=mc):
                                return t.matmul(PS[7][:, h2 * 65:(h2 + 1) * 65],
                                                qp_fm[:, (h2 * 2 + mc) * 128:(h2 * 2 + mc + 1) * 128],
                                                ssb[:, (h2 * 2 + mc) * 65:(h2 * 2 + mc + 1) * 65],
                                                start=False, stop=(mc == 1))
                            fns.append(op("tensor", fn3))
                fnum3 = fns[-1]
                prev_scp = t_scp
                fupd = []
                for h2 in range(2):
                    for mc in range(2):
                        if h2 == 0 and mc == 0:
                            wait("tensor", [t_kp, t_kp2])
                            wait("tensor", prev_scp)
                        fupd.append(op("tensor", lambda t, h2=h2, mc=mc, tau=tau: t.matmul(
                            PS[5][:, (h2 * 2 + mc) * 65:(h2 * 2 + mc + 1) * 65],
                            kp_tm3[:, (h2 * 2 + mc) * 128:(h2 * 2 + mc + 1) * 128],
                            vp1[:, h2 * 65:(h2 + 1) * 65],
                            start=(h2 == 0 and mc == 0),
                            stop=(h2 == 1 and mc == 1))))
                if tau < NT - 1:
                    wait("vector", fupd)
                    wait("vector", fns)
                    if tau == 0:
                        t_scp = op("vector", lambda v: v.tensor_copy(ssb[:, :], PS[5][:, 0:260]))
                    else:
                        t_scp = op("vector", lambda v: v.tensor_add(ssb[:, :], ssb[:, :], PS[5][:, 0:260]))
                if debug == "s3g" and g == 0 and tau == 0:
                    wait("vector", fupd)
                    wait("vector", fns)
                    return t_lnx + [op("vector", lambda v: v.tensor_copy(X[:, 0:130], PS[7][:, 0:130]))]
                wait("vector", fns)
                wait("vector", t_at_tr)
                wait("vector", tr3g.get("x"))
                tds3 = op("vector", lambda v: v.tensor_copy(
                    scl[:, 2:4],
                    PS[7][:, 0:130].rearrange("p (s c) -> p s c", c=65)[:, :, 64:65].rearrange(
                        "p s o -> p (s o)")))
                wait("vector", tds3)
                tr3 = op("vector", lambda v: v.reciprocal(scl[:, 4:6], scl[:, 2:4]))
                tr3g["x"] = tr3
                wait("vector", tr3)
                t_att = op("vector", lambda v: v.tensor_tensor(
                    attng[:, :].rearrange("p (s d) -> p s d", d=64),
                    PS[7][:, 0:130].rearrange("p (s c) -> p s c", c=65)[:, :, 0:64],
                    scl[:, 4:6].unsqueeze(2).broadcast_to([128, 2, 64]), OP.mult))
                wait("tensor", t_att)
                wait("tensor", [t_afh, t_afl])
                t_at_tr = op("tensor", lambda t: t.matmul(
                    PS[6][:, 256:384], attng[:, :], ident,
                    start=True, stop=True, is_transpose=True))
                wait("scalar", t_at_tr)
                wait("scalar", fop3)
                t_afh = op("scalar", lambda s: s.activation(afh, PS[6][:, 256:384], AF.Copy))
                wait("vector", [t_at_tr, t_afh])
                wait("vector", fop3)
                t_afl = op("vector", lambda v: v.tensor_tensor(
                    afl, PS[6][:, 256:384], afh, OP.subtract))
                fres3 = []
                fop3 = []
                for n in range(2):
                    wait("tensor", [t_afh, t_afl] if n == 0 else None)
                    wait("tensor", [t_wo3] if tau == 0 and n == 0 else None)
                    wait("tensor", fres3[-1] if fres3 else (
                        prev_fres3[-1] if prev_fres3 else None))
                    wait("tensor", [t_kpfm_ev] if n == 0 else None)
                    def fo3(t, n=n):
                        whi, wlo = osl
                        t.matmul(PS[2][:, n * 512:n * 512 + 512] if False else
                                 PS[2][:, 0:512], afh, whi[:, n * 512:(n + 1) * 512],
                                 start=True, stop=False)
                        t.matmul(PS[2][:, 0:512], afh, wlo[:, n * 512:(n + 1) * 512],
                                 start=False, stop=False)
                        return t.matmul(PS[2][:, 0:512], afl, whi[:, n * 512:(n + 1) * 512],
                                        start=False, stop=True)
                    fop3.append(op("tensor", fo3))
                    wait("vector", fop3[-1])
                    wait("vector", t_lnx[tau] if g == 0 else None)
                    fres3.append(op("vector", lambda v, tau=tau, n=n: v.scalar_tensor_tensor(
                        X[:, tau * E + n * 512:tau * E + n * 512 + 512], PS[2][:, 0:512],
                        1.0, X[:, tau * E + n * 512:tau * E + n * 512 + 512],
                        OP.mult, OP.add)))
                prev_fres3 = fres3
                t_res3.extend(fres3)
                if debug == "s3taps" and g == 0 and tau == 1:
                    wait("vector", [t_qf_ev, t_kf_ev, t_kp, t_kp2, t_qp, t_vev,
                                    t_att, fres3[-1], t_scp])
                    op("vector", lambda v: v.tensor_copy(X[:, 0:128], QFM[:, 0:128]))
                    op("vector", lambda v: v.tensor_copy(X[:, 128:256], KFM[:, 0:128]))
                    op("vector", lambda v: v.tensor_copy(X[:, 256:768], kp_tm3[:, :]))
                    op("vector", lambda v: v.tensor_copy(X[:, 768:1280], qp_fm[:, :]))
                    op("vector", lambda v: v.tensor_copy(X[:, 1280:1536], amask[:, :]))
                    op("vector", lambda v: v.tensor_copy(X[:, 1536:1666], vp1[:, :]))
                    op("vector", lambda v: v.tensor_copy(X[:, 1666:1794], attng[:, :]))
                    op("vector", lambda v: v.tensor_copy(X[:, 1794:2054], ssb[:, :]))
                    return [op("vector", lambda v: v.tensor_copy(X[:, 2054:2056], ncr[:, 0:2]))]

        if debug == "s3prepost":
            return t_res3
        wait("sync", t_res3[-4:])
        wait("sync", [t_kpfm_ev, t_qp, t_kp, t_kp2, t_scp, t_att, t_afh, t_afl])
        wait("sync", [fnum3, fupd[-1], fmq[-1], fat3[-1], t_kf_ev, t_wo3, t_ktev,
                      t_red3, t_ncr, t_vev, t_ones3])
        t_g = dma("sync", gam[:, :],
                  w_d["L0_attn_b_gamma"].unsqueeze(0).partition_broadcast(128))
        t_b = dma("sync", bet[:, :], w_d["L0_attn_b_beta"].unsqueeze(0).partition_broadcast(128))
        t_ob = dma("sync", outbt[:, :], w_d["L0_attn_b_out_b"].unsqueeze(0).partition_broadcast(128))
        t_addb = []
        for tau in range(NT):
            wait("vector", [t_g, t_b, t_ob] if tau == 0 else None)
            wait("vector", t_res3 if tau == 0 else None)
            t_addb.append(op("vector", lambda v, tau=tau: v.tensor_add(
                X[:, tau * E:(tau + 1) * E], X[:, tau * E:(tau + 1) * E], outbt[:, :])))
        t_stats4 = ln_stats(1e-6, deps=t_addb)
        t_out3 = []
        prev_n = None
        for tau in range(NT):
            wait("vector", t_stats4 if tau == 0 else None)
            t_n1 = op("vector", lambda v, tau=tau: v.tensor_scalar(
                X[:, tau * E:(tau + 1) * E], X[:, tau * E:(tau + 1) * E],
                mu[:, tau:tau + 1], rstd[:, tau:tau + 1], OP.subtract, OP.mult))
            wait("vector", t_n1)
            t_n2 = op("vector", lambda v, tau=tau: v.tensor_mul(
                X[:, tau * E:(tau + 1) * E], X[:, tau * E:(tau + 1) * E], gam[:, :]))
            wait("vector", t_n2)
            t_n3 = op("vector", lambda v, tau=tau: v.tensor_add(
                X[:, tau * E:(tau + 1) * E], X[:, tau * E:(tau + 1) * E], bet[:, :]))
            t_out3.append(t_n3)
        xo = [ln_tm[:, :], A32[:, 6680:7704]]
        t_xo = {0: None, 1: None}
        t_add = {0: None, 1: None}
        t_out4 = []
        for tau in range(NT):
            b = xo[tau % 2]
            wait("sync", t_stash[-1] if tau == 0 else None)
            wait("sync", t_add[tau % 2])
            if tau < 2:
                wait("sync", t_out3)
            t_xo[tau % 2] = dma("sync", b, h_scr[tau * 128:(tau + 1) * 128, 0:1024])
            wait("vector", [t_xo[tau % 2], t_out3[tau]])
            t_add[tau % 2] = op("vector", lambda v, tau=tau, b=b: v.tensor_add(
                X[:, tau * E:(tau + 1) * E], X[:, tau * E:(tau + 1) * E], b))
            t_out4.append(t_add[tau % 2])
        return t_out4

    t_final = [t_x]
    if stages >= 1:
        t_final = stage1()
    if stages >= 2:
        tg_dmas, t_wsg = prep_weight(0, "L0_ffn_w_gate", tern_g, t_final, t_final, t_final, t_final)
        tu_dmas, t_wsu = prep_weight(1, "L0_ffn_w_up", tern_u, tg_dmas[-1], None, None, None)
        td_dmas, t_wsd = prep_weight(2, "L0_ffn_w_down", tern_d, tu_dmas[-1], None, None, None)
        prep = {"g_dmas": tg_dmas, "u_dmas": tu_dmas, "d_dmas": td_dmas,
                "ln_tm_free": None, "afm_free": None}
        if debug != "prep":
            t_final = stage_ffn(0, t_final + [t_wsg, t_wsu, t_wsd, td_dmas[-1]], prep)
        else:
            t_final = t_final + [t_wsg, t_wsu, t_wsd, td_dmas[-1]]
    if stages >= 3:
        t_final = stage3(t_final)
    if stages >= 4:
        t_final = stage_ffn(1, t_final, prep)

    wait("sync", t_final)
    yout = y_d.rearrange("(a p) e -> p a e", p=128)
    t_y = p.dma("sync", yout, X[:, :].rearrange("p (a e) -> p a e", e=E), "st")
    wait("sync", t_y)

    with nc.Block() as block:
        p.run(block)
    ctx.close()
    return nc


def kernel(**inputs):
    xs = np.ascontiguousarray(np.asarray(inputs["L0_attn_a_x"], dtype=np.float32))
    nc = build_program()
    cst = const_inputs()
    in_maps = []
    for c in range(8):
        m = {"x": xs[c]}
        m.update(cst)
        for k in WEIGHT_KEYS:
            m[k] = np.ascontiguousarray(np.asarray(inputs[k], dtype=np.float32))
        in_maps.append(m)
    res = run_bass_kernel_spmd(nc, in_maps, core_ids=list(range(8)))
    return np.stack([np.asarray(r["y"], dtype=np.float32) for r in res.results], axis=0)
